# version 01 (final)
# baseline (speedup 1.0000x reference)
import numpy as np
import concourse.bass as bass
import concourse.mybir as mybir
from concourse.bass_utils import run_bass_kernel_spmd

F32 = mybir.dt.float32
BF16 = mybir.dt.bfloat16
AF = mybir.ActivationFunctionType
ALU = mybir.AluOpType
AX = mybir.AxisListType

P = 128
DM = 2048
NDC = 16
L = 2048
NTC = 16
NTO = 8
TO = 1024
HEADS = 16
G = 2
NG = 8
DH = 128
SCALE = DH ** -0.5
EPS_LN = 1e-5
EPS_RMS = 1e-6
NE = 16
NI = 11
SLOT = 2048
RING = 4
TOPK = 4

DEBUG = False


class Tape:
    def __init__(self):
        self.parts = []
        self.col = 0

    def tile(self, arr):
        arr = np.ascontiguousarray(arr, dtype=np.float32)
        assert arr.shape[0] == P
        w = arr.shape[1]
        assert w in (128, 256, 512, 1024), w
        if self.col % w:
            pad = w - self.col % w
            self.parts.append(np.zeros((P, pad), np.float32))
            self.col += pad
        c0 = self.col
        self.parts.append(arr)
        self.col += w
        assert c0 % SLOT + w <= SLOT
        return c0

    def finish(self):
        if self.col % SLOT:
            pad = SLOT - self.col % SLOT
            self.parts.append(np.zeros((P, pad), np.float32))
            self.col += pad
        return np.concatenate(self.parts, axis=1)


class Sem:
    __slots__ = ("h", "n")

    def __init__(self, nc, name):
        self.h = nc.alloc_semaphore(name)
        self.n = 0


def build(W):
    t_ = Tape()
    nc = bass.Bass("TRN2", target_bir_lowering=False)

    x_tm = nc.dram_tensor("x_tm", [L, DM], F32, kind="ExternalInput").ap()
    x_fm = nc.dram_tensor("x_fm", [DM, TO], F32, kind="ExternalInput").ap()
    bias_in = nc.dram_tensor("bias_in", [P, NDC], F32, kind="ExternalInput").ap()
    qkw_in = nc.dram_tensor("qkw_in", [P, 2], F32, kind="ExternalInput").ap()
    y_out = nc.dram_tensor("y_out", [DM, TO], F32, kind="ExternalOutput").ap()
    y_spill = nc.dram_tensor("y_spill", [DM, TO], F32, kind="Internal").ap()
    dbg = {}
    if DEBUG:
        dbg["ln1"] = nc.dram_tensor("d_ln1", [P, NDC * L], BF16, kind="ExternalOutput").ap()
        dbg["qf"] = nc.dram_tensor("d_qf", [P, G * TO], BF16, kind="ExternalOutput").ap()
        dbg["kf"] = nc.dram_tensor("d_kf", [P, G * L], BF16, kind="ExternalOutput").ap()
        dbg["vt"] = nc.dram_tensor("d_vt", [P, NTC * G * DH], BF16, kind="ExternalOutput").ap()
        dbg["of"] = nc.dram_tensor("d_of", [P, HEADS * TO], BF16, kind="ExternalOutput").ap()
        dbg["ao"] = nc.dram_tensor("d_ao", [P, NDC * TO], F32, kind="ExternalOutput").ap()
        dbg["ln2"] = nc.dram_tensor("d_ln2", [P, NDC * TO], BF16, kind="ExternalOutput").ap()
        dbg["coef"] = nc.dram_tensor("d_coef", [P, NTO * 17], BF16, kind="ExternalOutput").ap()
        dbg["mbb"] = nc.dram_tensor("d_mbb", [P, TO], F32, kind="ExternalOutput").ap()
        dbg["glog"] = nc.dram_tensor("d_glog", [P, 512], F32, kind="ExternalOutput").ap()
        dbg["rbb"] = nc.dram_tensor("d_rbb", [P, TO], F32, kind="ExternalOutput").ap()

    top = nc.sbuf_top
    cur = [nc.sbuf_base]

    def sb(name, shape, dtype, at=None):
        n = int(np.prod(shape[1:]))
        bpe = 2 if dtype == BF16 else 4
        off = at if at is not None else cur[0]
        off = (off + 63) // 64 * 64
        th = nc.alloc_sbuf_tensor_at(name, list(shape), dtype, offset=off)
        cur[0] = off + n * bpe
        assert cur[0] <= top, (name, cur[0], top)
        return th

    ring = sb("ring", [P, RING * SLOT], BF16)
    ident = sb("ident", [P, P], BF16)
    ones_c = sb("ones_c", [P, 1], F32)
    ones_rf = sb("ones_rf", [1, P], F32)
    ones_rb = sb("ones_rb", [1, P], BF16)
    qkw_s = sb("qkw_s", [P, 2], F32)
    wqk = sb("wqk", [P, 1], F32)
    biast = sb("biast", [P, NDC], F32)
    trash = sb("trash", [P, P], F32)
    epsln = sb("epsln", [P, 1], F32)
    epsrm = sb("epsrm", [P, 1], F32)
    rowA = sb("rowA", [1, TO], F32)
    rowB = sb("rowB", [1, TO], F32)
    mbb = sb("mbb", [P, TO], F32)
    rbb = sb("rbb", [P, TO], F32)
    st1 = sb("st1", [P, 8], F32)
    qss = sb("qss", [P, 3 * NTO * G], F32)
    kss = sb("kss", [P, 3 * NTC * G], F32)
    nmax = sb("nmax", [P, 2], F32)
    rsum = sb("rsum", [P, 2], F32)
    rinv = sb("rinv", [P, 2], F32)
    glog = sb("glog", [P, 512], F32)
    gprob = sb("gprob", [P, NTO * NE], F32)
    gwork = sb("gwork", [P, NTO * NE], F32)
    gmask = sb("gmask", [P, NTO * NE], F32)
    gm = sb("gm", [P, NTO], F32)
    gnm = sb("gnm", [P, NTO], F32)
    grs = sb("grs", [P, NTO], F32)
    gri = sb("gri", [P, NTO], F32)
    coefb = sb("coefb", [P, NTO * 17], BF16)
    crow = sb("crow", [1, TO], BF16)
    cbb = [sb("cb0", [P, TO], BF16), sb("cb1", [P, TO], BF16)]
    sgb = sb("sgb", [P, TO], BF16)
    sgl = sb("sgl", [P, 1024], BF16)
    sgl2 = sb("sgl2", [P, 1024], BF16)

    r1 = cur[0]
    ln1f = sb("ln1f", [P, NDC * L], BF16)
    attn_o = sb("attn_o", [P, NDC * TO], F32, at=r1)
    ffn_a = sb("ffn_a", [P, NDC * TO], F32, at=r1)

    r2 = cur[0]
    o_fm = sb("o_fm", [P, HEADS * TO], BF16)
    ln2f = sb("ln2f", [P, NDC * TO], BF16, at=r2)

    r3 = cur[0]
    xt = [sb("xt0", [P, DM], F32), sb("xt1", [P, DM], F32)]
    xsq = sb("xsq", [P, DM], F32)
    lnt = [sb("lnt0", [P, DM], BF16), sb("lnt1", [P, DM], BF16)]
    e_p1 = cur[0]
    q_tm = sb("q_tm", [P, NTO * G * DH], BF16, at=r3)
    k_tm = sb("k_tm", [P, NTC * G * DH], BF16)
    v_tm = sb("v_tm", [P, NTC * G * DH], BF16)
    q_fm = sb("q_fm", [P, G * TO], BF16)
    k_fm = sb("k_fm", [P, G * L], BF16)
    pbuf = [sb("pb0", [P, L], BF16), sb("pb1", [P, L], BF16)]
    p_T = sb("p_T", [P, NTC * 256], BF16)
    e_p23 = cur[0]
    sqb = [sb("sq0", [P, TO], F32, at=r3), sb("sq1", [P, TO], F32)]
    xf = [sb("xf0", [P, TO], F32), sb("xf1", [P, TO], F32)]
    ybv = [sb("yb0", [P, TO], F32), sb("yb1", [P, TO], F32)]
    tta = sb("tta", [P, TO], F32)
    e_p46 = cur[0]
    hb = [sb("h0", [P, NI * 1024], BF16, at=r3), sb("h1", [P, NI * 1024], BF16)]
    e_p89 = cur[0]
    y2b = ybv
    sqc = sqb
    yrb = xf
    tt2 = tta
    assert max(e_p1, e_p23, e_p46, e_p89) <= top, (e_p1, e_p23, e_p46, e_p89, top)

    S_ps = nc.alloc_psum_tensor("S_ps", [P, 2048], F32)
    M_ps = nc.alloc_psum_tensor("M_ps", [P, 1024], F32)
    T1_ps = nc.alloc_psum_tensor("T1_ps", [P, 1024], BF16)
    T2_ps = nc.alloc_psum_tensor("T2_ps", [P, 1024], BF16)

    steps = {"pe": [], "ve": [], "se": [], "gp": [], "sy": []}
    S = {n: Sem(nc, n) for n in ["VE", "SE", "PE", "ring_ld", "xdma"]}
    last = {"ve": None, "se": None}
    rstt = {"waited": 0, "rel_assigned": 0, "consumed": 0}
    chunk_rel = {}

    def _wait(eng, mark):
        sem, val = mark
        steps[eng].append(lambda e, h=sem.h, v=val: e.wait_ge(h, v))

    def _op(eng, fn, milestone=None):
        if eng in ("ve", "se") and last[eng] is not None:
            _wait(eng, last[eng])
        sem = milestone if milestone is not None else (
            S["VE"] if eng == "ve" else S["SE"] if eng == "se" else S["PE"])
        sem.n += 1
        steps[eng].append(lambda e, fn=fn, h=sem.h: fn(e).then_inc(h, 1))
        mk = (sem, sem.n)
        if eng in ("ve", "se"):
            last[eng] = mk
        return mk

    def ve(fn):
        return _op("ve", fn)

    def se(fn):
        return _op("se", fn)

    def pe(fn):
        mk = _op("pe", fn)
        while rstt["rel_assigned"] < rstt["consumed"]:
            chunk_rel[rstt["rel_assigned"]] = mk
            rstt["rel_assigned"] += 1
        return mk

    def pe_raw(fn):
        steps["pe"].append(lambda e, fn=fn: fn(e))

    def vwait(mark):
        _wait("ve", mark)

    def swait(mark):
        _wait("se", mark)

    def pwait(mark):
        _wait("pe", mark)

    def sywait(mark):
        _wait("sy", mark)

    def sy_dma(out, in_):
        S["xdma"].n += 16
        steps["sy"].append(lambda e, o=out, i=in_: e.dma_start(out=o, in_=i).then_inc(S["xdma"].h, 16))
        return (S["xdma"], S["xdma"].n)

    def rv(c0, w):
        c = c0 // SLOT
        off = c0 % SLOT
        assert off + w <= SLOT
        if c >= rstt["waited"]:
            pwait((S["ring_ld"], 16 * (c + 1)))
            rstt["waited"] = c + 1
        return ring[:, (c % RING) * SLOT + off: (c % RING) * SLOT + off + w]

    def rdone(col_end):
        rstt["consumed"] = max(rstt["consumed"], col_end // SLOT)

    BANKS = {"SA": {0, 1}, "SB": {2, 3}, "Sfull": {0, 1, 2, 3},
             "SM": {4, 5}, "AV0": {4}, "AV1": {5}, "MST": {4, 5},
             "GT": {4}, "DN0": {4}, "DN1": {5}, "T1": {6}, "T2": {7},
             "BC1": {0, 1}, "BC2": {2, 3}}
    pslot = {}

    def slot_acquire(name):
        for other, d in list(pslot.items()):
            if BANKS[other] & BANKS[name] and d.get("evac"):
                pwait(d["evac"])
                d["evac"] = None

    def slot_fill(name, mark):
        pslot.setdefault(name, {})["fill"] = mark

    def slot_fillmark(name):
        return pslot[name]["fill"]

    def slot_evac(name, mark):
        pslot.setdefault(name, {})["evac"] = mark

    steps["gp"].append(lambda g: g.memset(ident[:, :], 0.0))
    steps["gp"].append(lambda g: g.affine_select(
        out=ident[:, :], in_=ident[:, :], compare_op=ALU.not_equal, fill=1.0,
        base=0, pattern=[[-1, P]], channel_multiplier=1))
    steps["gp"].append(lambda g: g.memset(ones_c[:, :], 1.0))
    steps["gp"].append(lambda g: g.memset(ones_rf[:, :], 1.0))
    steps["gp"].append(lambda g: g.memset(ones_rb[:, :], 1.0))
    steps["gp"].append(lambda g: g.memset(epsln[:, :], EPS_LN))
    S["gpc"] = Sem(nc, "gpc")
    S["gpc"].n = 1
    steps["gp"].append(lambda g: g.memset(epsrm[:, :], EPS_RMS).then_inc(S["gpc"].h, 1))
    pwait((S["gpc"], 1))

    m_qkw = sy_dma(qkw_s[:, :], qkw_in[:, :])
    m_bias = sy_dma(biast[:, :], bias_in[:, :])
    vwait(m_qkw)
    ve(lambda v: v.tensor_mul(wqk[:, :], qkw_s[:, 0:1], qkw_s[:, 1:2]))

    lnT_marks = []
    xt_free = []
    for tc in range(NTC):
        xb = xt[tc % 2]
        lb = lnt[tc % 2]
        if tc >= 2:
            sywait(xt_free[tc - 2])
        m_x = sy_dma(xb[:, :], x_tm[tc * P:(tc + 1) * P, :])
        swait(m_x)
        m_sq = se(lambda sc, xb=xb: sc.activation(out=xsq[:, :], in_=xb[:, :], func=AF.Square))
        vwait(m_x)
        ve(lambda v, xb=xb: v.reduce_sum(st1[:, 0:1], xb[:, :], axis=AX.X))
        vwait(m_sq)
        ve(lambda v: v.reduce_sum(st1[:, 1:2], xsq[:, :], axis=AX.X))
        ve(lambda v: v.tensor_scalar(out=st1[:, 2:3], in0=st1[:, 0:1],
                                     scalar1=1.0 / DM, scalar2=None, op0=ALU.mult))
        ve(lambda v: v.tensor_mul(st1[:, 3:4], st1[:, 2:3], st1[:, 2:3]))
        m_var = ve(lambda v: v.tensor_scalar(out=st1[:, 4:5], in0=st1[:, 1:2],
                                             scalar1=1.0 / DM, scalar2=st1[:, 3:4],
                                             op0=ALU.mult, op1=ALU.subtract))
        swait(m_var)
        m_sd = se(lambda sc: sc.activation(out=st1[:, 5:6], in_=st1[:, 4:5],
                                           func=AF.Sqrt, bias=epsln[:, 0:1]))
        vwait(m_sd)
        ve(lambda v: v.reciprocal(st1[:, 6:7], st1[:, 5:6]))
        if tc >= 2:
            vwait(lnT_marks[2 * (tc - 2) + 1])
        m_ap = ve(lambda v, xb=xb, lb=lb: v.tensor_scalar(
            out=lb[:, :], in0=xb[:, :], scalar1=st1[:, 2:3], scalar2=st1[:, 6:7],
            op0=ALU.subtract, op1=ALU.mult))
        xt_free.append(m_ap)
        for half in range(2):
            tname = "T1" if half == 0 else "T2"
            Tp = T1_ps if half == 0 else T2_ps
            slot_acquire(tname)
            pwait(m_ap)
            for j in range(8):
                dc = half * 8 + j
                fn = (lambda e, Tp=Tp, j=j, lb=lb, dc=dc:
                      e.transpose(Tp[:, j * P:(j + 1) * P],
                                  lb[:, dc * P:(dc + 1) * P], ident[:, :]))
                if j == 7:
                    slot_fill(tname, pe(fn))
                else:
                    pe_raw(fn)
            vwait(slot_fillmark(tname))
            m_ev = ve(lambda v, Tp=Tp, half=half, tc=tc: v.tensor_copy(
                bass.AP(ln1f, half * 8 * L + tc * P, [[NDC * L, P], [L, 8], [1, P]]),
                bass.AP(Tp, 0, [[1024, P], [P, 8], [1, P]])))
            lnT_marks.append(m_ev)
            slot_evac(tname, m_ev)
    m_dln1 = None
    if DEBUG:
        sywait(lnT_marks[-1])
        m_dln1 = sy_dma(dbg["ln1"][:, :], ln1f[:, :])

    def ln1v(dc, t0, w):
        return ln1f[:, dc * L + t0: dc * L + t0 + w]

    def ln2v(dc, t0, w):
        return ln2f[:, dc * TO + t0: dc * TO + t0 + w]

    wqkvT = np.asarray(W["w_qkv"]).T
    wprojT = np.asarray(W["w_proj"]).T
    gatesT = np.concatenate([np.asarray(W["gate_w"]).T,
                             np.asarray(W["sgate_w"]).T], axis=1)
    sguT = np.asarray(W["shared_gate_up_w"]).T
    sdwT = np.asarray(W["shared_down_w"]).T
    w13T = np.asarray(W["w13"]).transpose(0, 2, 1)
    w2T = np.asarray(W["w2"]).transpose(0, 2, 1)

    qkv_rot = [0]
    tpo_rot = [0]
    av_rot = [0]
    dn_rot = [0]
    up_rot = [0]
    o_marks = []
    unit_pT = [None]

    def qkv_part(cols, ntc, out_sb):
        W_ = G * DH
        c0s = [t_.tile(cols[di * P:(di + 1) * P, :]) for di in range(NDC)]
        marks = []
        nblk = ntc // 2
        for blk in range(nblk):
            sname = ["SA", "SB", "SM"][qkv_rot[0] % 3]
            qkv_rot[0] += 1
            ps = {"SA": S_ps[:, 0:1024], "SB": S_ps[:, 1024:2048],
                  "SM": M_ps[:, :]}[sname]
            slot_acquire(sname)
            m_f = None
            for di in range(NDC):
                w_ap = rv(c0s[di], W_)
                for t2 in range(2):
                    tcc = blk * 2 + t2
                    fn = (lambda e, ps=ps, t2=t2, di=di, w_ap=w_ap, tcc=tcc, W_=W_:
                          e.matmul(ps[:, t2 * 512:t2 * 512 + W_],
                                   ln1v(di, tcc * P, P), w_ap,
                                   start=(di == 0), stop=(di == NDC - 1)))
                    if di == NDC - 1 and t2 == 1:
                        m_f = pe(fn)
                        slot_fill(sname, m_f)
                    else:
                        pe_raw(fn)
                if blk == nblk - 1:
                    rdone(c0s[di] + W_)
            swait(m_f)
            m_ev = se(lambda sc, ps=ps, out_sb=out_sb, blk=blk, W_=W_: sc.activation(
                out=bass.AP(out_sb, blk * 2 * W_, [[out_sb.shape[1], P], [W_, 2], [1, W_]]),
                in_=bass.AP(ps.tensor, ps.offset, [ps.ap[0], [512, 2], [1, W_]]),
                func=AF.Copy))
            marks.append(m_ev)
            slot_evac(sname, m_ev)
        return marks

    def rms_norm(ss, tm, ntc, marks):
        for m in marks:
            swait(m)
        nsl = ntc * G
        for sl in range(nsl):
            se(lambda sc, sl=sl, tm=tm, ss=ss: sc.activation(
                out=trash[:, :], in_=tm[:, sl * DH:(sl + 1) * DH],
                func=AF.Square, accum_out=ss[:, sl:sl + 1]))
        m_sd = se(lambda sc, ss=ss, nsl=nsl: sc.activation(
            out=ss[:, nsl:2 * nsl], in_=ss[:, 0:nsl],
            func=AF.Sqrt, bias=epsrm[:, 0:1], scale=1.0 / DH))
        vwait(m_sd)
        ve(lambda v, ss=ss, nsl=nsl: v.reciprocal(
            ss[:, 2 * nsl:3 * nsl], ss[:, nsl:2 * nsl]))
        m = None
        for sl in range(nsl):
            m = ve(lambda v, sl=sl, tm=tm, ss=ss, nsl=nsl: v.tensor_scalar(
                out=tm[:, sl * DH:(sl + 1) * DH], in0=tm[:, sl * DH:(sl + 1) * DH],
                scalar1=ss[:, 2 * nsl + sl:2 * nsl + sl + 1],
                scalar2=None, op0=ALU.mult))
        return m

    def tposes(tm, fm, ntc, rdy, scale_ap):
        m_ev = None
        for h2 in range(G):
            for bat in range(ntc // 8):
                tname = "T1" if tpo_rot[0] % 2 == 0 else "T2"
                Tp = T1_ps if tname == "T1" else T2_ps
                tpo_rot[0] += 1
                slot_acquire(tname)
                pwait(rdy)
                m_f = None
                for j in range(8):
                    tcc = bat * 8 + j
                    fn = (lambda e, Tp=Tp, j=j, tm=tm, tcc=tcc, h2=h2:
                          e.transpose(Tp[:, j * P:(j + 1) * P],
                                      tm[:, tcc * G * DH + h2 * DH: tcc * G * DH + (h2 + 1) * DH],
                                      ident[:, :]))
                    if j == 7:
                        m_f = pe(fn)
                        slot_fill(tname, m_f)
                    else:
                        pe_raw(fn)
                dst = fm[:, h2 * ntc * P + bat * 8 * P: h2 * ntc * P + (bat + 1) * 8 * P]
                if scale_ap is not None:
                    swait(m_f)
                    m_ev = se(lambda sc, Tp=Tp, dst=dst: sc.activation(
                        out=dst, in_=Tp[:, :], func=AF.Copy, scale=scale_ap))
                else:
                    vwait(m_f)
                    m_ev = ve(lambda v, Tp=Tp, dst=dst: v.tensor_copy(dst, Tp[:, :]))
                slot_evac(tname, m_ev)
        return m_ev

    for g in range(NG):
        qcols = wqkvT[:, g * 256:(g + 1) * 256]
        kcols = wqkvT[:, 2048 + g * 256: 2048 + (g + 1) * 256]
        vcols = wqkvT[:, 4096 + g * 256: 4096 + (g + 1) * 256]
        q_marks = qkv_part(qcols, NTO, q_tm)
        k_marks = qkv_part(kcols, NTC, k_tm)
        v_marks = qkv_part(vcols, NTC, v_tm)
        m_q = rms_norm(qss, q_tm, NTO, q_marks)
        m_k = rms_norm(kss, k_tm, NTC, k_marks)
        m_qT = tposes(q_tm, q_fm, NTO, m_q, wqk[:, 0:1])
        m_kT = tposes(k_tm, k_fm, NTC, m_k, None)

        if DEBUG and g == 0:
            sywait(m_qT)
            sy_dma(dbg["qf"][:, :], q_fm[:, :])
            sywait(m_kT)
            sy_dma(dbg["kf"][:, :], k_fm[:, :])
            sywait(v_marks[-1])
            m_dmp = sy_dma(dbg["vt"][:, :], v_tm[:, :])
            swait(m_dmp)
            vwait(m_dmp)

        pwait(m_qT)
        pwait(m_kT)
        pwait(v_marks[-1])
        for h2 in range(G):
            for qb4 in range(4):
                prev_pT = unit_pT[0]
                pT_marks = []
                for qc in range(2):
                    slot_acquire("Sfull")
                    m_S = None
                    for kb in range(4):
                        fn = (lambda e, kb=kb, h2=h2, qb4=qb4, qc=qc:
                              e.matmul(S_ps[:, kb * 512:(kb + 1) * 512],
                                       q_fm[:, h2 * TO + qb4 * 256 + qc * P:
                                            h2 * TO + qb4 * 256 + (qc + 1) * P],
                                       k_fm[:, h2 * L + kb * 512: h2 * L + (kb + 1) * 512],
                                       start=True, stop=True))
                        if kb == 3:
                            m_S = pe(fn)
                            slot_fill("Sfull", m_S)
                        else:
                            pe_raw(fn)
                    vwait(m_S)
                    ve(lambda v: v.reduce_max(nmax[:, 0:1], S_ps[:, :], axis=AX.X, negate=True))
                    m_nm = ve(lambda v: v.tensor_scalar(out=nmax[:, 1:2], in0=nmax[:, 0:1],
                                                        scalar1=SCALE, scalar2=None, op0=ALU.mult))
                    pbv = pbuf[qc % 2]
                    swait(m_nm)
                    if prev_pT is not None:
                        swait(prev_pT[2 * qc + 1])
                    m_exp = se(lambda sc, pbv=pbv: sc.activation(
                        out=pbv[:, :], in_=S_ps[:, :], func=AF.Exp,
                        bias=nmax[:, 1:2], scale=SCALE, accum_out=rsum[:, 0:1]))
                    slot_evac("Sfull", m_exp)
                    vwait(m_exp)
                    ve(lambda v: v.reciprocal(rinv[:, 0:1], rsum[:, 0:1]))
                    m_pn = ve(lambda v, pbv=pbv: v.tensor_scalar(
                        out=pbv[:, :], in0=pbv[:, :], scalar1=rinv[:, 0:1],
                        scalar2=None, op0=ALU.mult))
                    for bat in range(2):
                        tname = "T1" if tpo_rot[0] % 2 == 0 else "T2"
                        Tp = T1_ps if tname == "T1" else T2_ps
                        tpo_rot[0] += 1
                        slot_acquire(tname)
                        pwait(m_pn)
                        m_f = None
                        for j in range(8):
                            kc = bat * 8 + j
                            fn = (lambda e, Tp=Tp, j=j, pbv=pbv, kc=kc:
                                  e.transpose(Tp[:, j * P:(j + 1) * P],
                                              pbv[:, kc * P:(kc + 1) * P], ident[:, :]))
                            if j == 7:
                                m_f = pe(fn)
                                slot_fill(tname, m_f)
                            else:
                                pe_raw(fn)
                        vwait(m_f)
                        m_ev = ve(lambda v, Tp=Tp, bat=bat, qc=qc: v.tensor_copy(
                            bass.AP(p_T, bat * 8 * 256 + qc * P, [[NTC * 256, P], [256, 8], [1, P]]),
                            bass.AP(Tp, 0, [[1024, P], [P, 8], [1, P]])))
                        pT_marks.append(m_ev)
                        slot_evac(tname, m_ev)
                avname = "AV%d" % (av_rot[0] % 2)
                avps = M_ps[:, 0:256] if av_rot[0] % 2 == 0 else M_ps[:, 512:768]
                av_rot[0] += 1
                slot_acquire(avname)
                pwait(pT_marks[-1])
                m_av = None
                for kc in range(NTC):
                    fn = (lambda e, kc=kc, h2=h2, avps=avps:
                          e.matmul(avps,
                                   v_tm[:, kc * G * DH + h2 * DH: kc * G * DH + (h2 + 1) * DH],
                                   p_T[:, kc * 256:(kc + 1) * 256],
                                   start=(kc == 0), stop=(kc == NTC - 1)))
                    if kc == NTC - 1:
                        m_av = pe(fn)
                        slot_fill(avname, m_av)
                    else:
                        pe_raw(fn)
                swait(m_av)
                hglob = g * G + h2
                m_o = se(lambda sc, avps=avps, hglob=hglob, qb4=qb4: sc.activation(
                    out=o_fm[:, hglob * TO + qb4 * 256: hglob * TO + (qb4 + 1) * 256],
                    in_=avps, func=AF.Copy))
                slot_evac(avname, m_o)
                o_marks.append(m_o)
                unit_pT[0] = pT_marks

    m_dof = None
    if DEBUG:
        sywait(o_marks[-1])
        m_dof = sy_dma(dbg["of"][:, :], o_fm[:, :])

    vwait(m_bias)
    pwait(o_marks[-1])
    if DEBUG:
        vwait(m_dln1)
    c0s_pj = {}
    for do in range(NDC):
        for di in range(NDC):
            c0s_pj[(do, di)] = t_.tile(wprojT[di * P:(di + 1) * P, do * P:(do + 1) * P])
    pj_marks = []
    pj_rot = [0]
    for do in range(NDC):
        sname = ["SA", "SB"][pj_rot[0] % 2]
        pj_rot[0] += 1
        ps = S_ps[:, 0:1024] if sname == "SA" else S_ps[:, 1024:2048]
        slot_acquire(sname)
        m_f = None
        for di in range(NDC):
            w_ap = rv(c0s_pj[(do, di)], P)
            for tb in range(2):
                fn = (lambda e, ps=ps, tb=tb, di=di, w_ap=w_ap:
                      e.matmul(ps[:, tb * 512:(tb + 1) * 512], w_ap,
                               o_fm[:, di * TO + tb * 512: di * TO + (tb + 1) * 512],
                               start=(di == 0), stop=(di == NDC - 1)))
                if di == NDC - 1 and tb == 1:
                    m_f = pe(fn)
                    slot_fill(sname, m_f)
                else:
                    pe_raw(fn)
            rdone(c0s_pj[(do, di)] + P)
        vwait(m_f)
        m_ev = ve(lambda v, ps=ps, do=do: v.tensor_scalar(
            out=attn_o[:, do * TO:(do + 1) * TO], in0=ps[:, :],
            scalar1=biast[:, do:do + 1], scalar2=None, op0=ALU.add))
        pj_marks.append(m_ev)
        slot_evac(sname, m_ev)

    def sum_mms(srcbuf, dc, nchunks):
        m = None
        for tb in range(2):
            fn = (lambda e, srcbuf=srcbuf, dc=dc, tb=tb, nchunks=nchunks:
                  e.matmul(M_ps[0:1, tb * 512:tb * 512 + 512], ones_c[:, :],
                           bass.AP(srcbuf.tensor, srcbuf.offset + tb * 512,
                                   [srcbuf.ap[0], [1, 512]]),
                           start=(dc == 0), stop=(dc == nchunks - 1)))
            if tb == 1:
                m = pe(fn)
                if dc == nchunks - 1:
                    slot_fill("DN1", m)
                    slot_fill("DN0", m)
            else:
                pe_raw(fn)
        return m

    def rows_evac(dst, fill_mark):
        vwait(fill_mark)
        ve(lambda v, dst=dst: v.tensor_copy(dst[0:1, 0:512], M_ps[0:1, 0:512]))
        m = ve(lambda v, dst=dst: v.tensor_copy(dst[0:1, 512:1024], M_ps[0:1, 512:1024]))
        slot_evac("DN0", m)
        slot_evac("DN1", m)
        return m

    def moments_and_bcast():
        ve(lambda v: v.tensor_scalar(out=rowA[:, :], in0=rowA[:, :],
                                     scalar1=1.0 / DM, scalar2=None, op0=ALU.mult))
        ve(lambda v: v.tensor_scalar(out=rowB[:, :], in0=rowB[:, :],
                                     scalar1=1.0 / DM, scalar2=None, op0=ALU.mult))
        ve(lambda v: v.tensor_mul(mbb[0:1, :], rowA[:, :], rowA[:, :]))
        m_v = ve(lambda v: v.tensor_sub(rowB[:, :], rowB[:, :], mbb[0:1, :]))
        swait(m_v)
        m_sd = se(lambda sc: sc.activation(out=rowB[:, :], in_=rowB[:, :],
                                           func=AF.Sqrt, bias=epsln[0:1, 0:1]))
        vwait(m_sd)
        m_rr = ve(lambda v: v.reciprocal(rowB[:, :], rowB[:, :]))
        slot_acquire("BC1")
        slot_acquire("BC2")
        pwait(m_rr)
        m1 = m2 = None
        for tb in range(2):
            fn = (lambda e, tb=tb: e.matmul(S_ps[:, tb * 512:(tb + 1) * 512],
                                            ones_rf[0:1, :], rowA[0:1, tb * 512:(tb + 1) * 512],
                                            start=True, stop=True))
            if tb == 1:
                m1 = pe(fn)
            else:
                pe_raw(fn)
            fn = (lambda e, tb=tb: e.matmul(S_ps[:, 1024 + tb * 512: 1024 + (tb + 1) * 512],
                                            ones_rf[0:1, :], rowB[0:1, tb * 512:(tb + 1) * 512],
                                            start=True, stop=True))
            if tb == 1:
                m2 = pe(fn)
            else:
                pe_raw(fn)
        slot_fill("BC1", m1)
        slot_fill("BC2", m2)
        vwait(m2)
        ve(lambda v: v.tensor_copy(mbb[:, :], S_ps[:, 0:1024]))
        m_bc = ve(lambda v: v.tensor_copy(rbb[:, :], S_ps[:, 1024:2048]))
        slot_evac("BC1", m_bc)
        slot_evac("BC2", m_bc)
        return m_bc

    slot_acquire("DN0")
    slot_acquire("DN1")
    pwait(pj_marks[-1])
    m_st = None
    for dc in range(NDC):
        m_st = sum_mms(attn_o[:, dc * TO:(dc + 1) * TO], dc, NDC)
    rows_evac(rowA, m_st)
    slot_acquire("DN0")
    slot_acquire("DN1")
    sqm = []
    swait(pj_marks[-1])
    for dc in range(NDC):
        if dc >= 2:
            swait(sqm[dc - 2][1])
        m_q = se(lambda sc, dc=dc: sc.activation(
            out=sqb[dc % 2][:, :], in_=attn_o[:, dc * TO:(dc + 1) * TO], func=AF.Square))
        pwait(m_q)
        m_st = sum_mms(sqb[dc % 2][:, :], dc, NDC)
        sqm.append((m_q, m_st))
    rows_evac(rowB, m_st)
    m_bc1 = moments_and_bcast()
    if DEBUG:
        sywait(m_bc1)
        sy_dma(dbg["mbb"][:, :], mbb[:, :])
        m_dmr = sy_dma(dbg["rbb"][:, :], rbb[:, :])
        vwait(m_dmr)
    m_dao = None
    if DEBUG:
        sywait(pj_marks[-1])
        m_dao = sy_dma(dbg["ao"][:, :], attn_o[:, :])

    sywait(o_marks[-1])
    slot_acquire("DN0")
    slot_acquire("DN1")
    y_marks = []
    spill_marks = []
    ymm_marks = []
    for dc in range(NDC):
        if dc >= 2:
            sywait(y_marks[dc - 2])
        m_xf = sy_dma(xf[dc % 2][:, :], x_fm[dc * P:(dc + 1) * P, :])
        if dc == 0:
            vwait(m_bc1)
        vwait(pj_marks[dc])
        if dc >= 2:
            vwait((S["xdma"], spill_marks[dc - 2][1]))
            vwait(ymm_marks[dc - 2])
        ve(lambda v, dc=dc: v.tensor_sub(tta[:, :], attn_o[:, dc * TO:(dc + 1) * TO], mbb[:, :]))
        ve(lambda v: v.tensor_mul(tta[:, :], tta[:, :], rbb[:, :]))
        vwait(m_xf)
        m_y = ve(lambda v, dc=dc: v.tensor_add(ybv[dc % 2][:, :], tta[:, :], xf[dc % 2][:, :]))
        y_marks.append(m_y)
        sywait(m_y)
        spill_marks.append(sy_dma(y_spill[dc * P:(dc + 1) * P, :], ybv[dc % 2][:, :]))
        pwait(m_y)
        ymm_marks.append(sum_mms(ybv[dc % 2][:, :], dc, NDC))
    rows_evac(rowA, ymm_marks[-1])
    slot_acquire("DN0")
    slot_acquire("DN1")
    sywait(spill_marks[-1])
    sywait(ymm_marks[-1])
    sq2m = []
    for dc in range(NDC):
        if dc >= 2:
            sywait(sq2m[dc - 2][0])
        m_yr = sy_dma(ybv[dc % 2][:, :], y_spill[dc * P:(dc + 1) * P, :])
        swait(m_yr)
        if dc >= 2:
            swait(sq2m[dc - 2][1])
        m_q = se(lambda sc, dc=dc: sc.activation(
            out=sqb[dc % 2][:, :], in_=ybv[dc % 2][:, :], func=AF.Square))
        pwait(m_q)
        m_st = sum_mms(sqb[dc % 2][:, :], dc, NDC)
        sq2m.append((m_q, m_st))
    rows_evac(rowB, m_st)

    m_bc2 = moments_and_bcast()
    ln2_marks = []
    sywait(sq2m[-1][0])
    sywait(sq2m[-1][1])
    for dc in range(NDC):
        if dc >= 2:
            sywait(ln2_marks[dc - 2])
        m_yr = sy_dma(ybv[dc % 2][:, :], y_spill[dc * P:(dc + 1) * P, :])
        if dc == 0:
            vwait(m_bc2)
            if DEBUG:
                vwait(m_dof)
        vwait(m_yr)
        ve(lambda v, dc=dc: v.tensor_sub(tta[:, :], ybv[dc % 2][:, :], mbb[:, :]))
        m_l2 = ve(lambda v, dc=dc: v.tensor_mul(ln2v(dc, 0, TO), tta[:, :], rbb[:, :]))
        ln2_marks.append(m_l2)
    if DEBUG:
        sywait(ln2_marks[-1])
        sy_dma(dbg["ln2"][:, :], ln2f[:, :])

    c0s_gt = [t_.tile(np.pad(gatesT[di * P:(di + 1) * P, :], ((0, 0), (0, 111))))
              for di in range(NDC)]
    pwait(ln2_marks[-1])
    gt_marks = []
    for tcc in range(NTO):
        sname = ["DN0", "DN1"][dn_rot[0] % 2]
        dn_rot[0] += 1
        ps = M_ps[:, 0:512] if sname == "DN0" else M_ps[:, 512:1024]
        slot_acquire(sname)
        m_gt = None
        for di in range(NDC):
            w_ap = rv(c0s_gt[di], P)
            fn = (lambda e, di=di, tcc=tcc, w_ap=w_ap, ps=ps:
                  e.matmul(bass.AP(ps.tensor, ps.offset, [ps.ap[0], [1, 17]]),
                           ln2v(di, tcc * P, P),
                           bass.AP(w_ap.tensor, w_ap.offset, [w_ap.ap[0], [1, 17]]),
                           start=(di == 0), stop=(di == NDC - 1)))
            if di == NDC - 1:
                m_gt = pe(fn)
                slot_fill(sname, m_gt)
            else:
                pe_raw(fn)
            if tcc == NTO - 1:
                rdone(c0s_gt[di] + P)
        vwait(m_gt)
        m_gl = ve(lambda v, tcc=tcc, ps=ps: v.tensor_copy(
            glog[:, tcc * 64: tcc * 64 + 17],
            bass.AP(ps.tensor, ps.offset, [ps.ap[0], [1, 17]])))
        gt_marks.append(m_gl)
        slot_evac(sname, m_gl)
    m_gl = gt_marks[-1]
    if DEBUG:
        sywait(m_gl)
        m_dgl = sy_dma(dbg["glog"][:, :], glog[:, :])
        vwait(m_dgl)
    ve(lambda v: v.tensor_copy(gprob[:, :], bass.AP(glog, 0, [[512, P], [64, NTO], [1, NE]])))
    m_gnm = ve(lambda v: v.reduce_max(gnm[:, :], bass.AP(gprob, 0, [[NTO * NE, P], [NE, NTO], [1, NE]]),
                                      axis=AX.X, negate=True))
    swait(m_gnm)
    m_e = None
    for t in range(NTO):
        m_e = se(lambda sc, t=t: sc.activation(
            out=gprob[:, t * NE:(t + 1) * NE], in_=gprob[:, t * NE:(t + 1) * NE],
            func=AF.Exp, bias=gnm[:, t:t + 1], accum_out=grs[:, t:t + 1]))
    vwait(m_e)
    ve(lambda v: v.reciprocal(gri[:, :], grs[:, :]))
    for t in range(NTO):
        ve(lambda v, t=t: v.tensor_scalar(out=gprob[:, t * NE:(t + 1) * NE],
                                          in0=gprob[:, t * NE:(t + 1) * NE],
                                          scalar1=gri[:, t:t + 1], scalar2=None, op0=ALU.mult))
    ve(lambda v: v.tensor_copy(gwork[:, :], gprob[:, :]))
    for r in range(TOPK - 1):
        ve(lambda v: v.reduce_max(gm[:, :], bass.AP(gwork, 0, [[NTO * NE, P], [NE, NTO], [1, NE]]),
                                  axis=AX.X))
        for t in range(NTO):
            ve(lambda v, t=t: v.tensor_scalar(out=gmask[:, t * NE:(t + 1) * NE],
                                              in0=gwork[:, t * NE:(t + 1) * NE],
                                              scalar1=gm[:, t:t + 1], scalar2=None, op0=ALU.is_ge))
            ve(lambda v, t=t: v.scalar_tensor_tensor(out=gwork[:, t * NE:(t + 1) * NE],
                                                     in0=gmask[:, t * NE:(t + 1) * NE], scalar=-1e9,
                                                     in1=gwork[:, t * NE:(t + 1) * NE],
                                                     op0=ALU.mult, op1=ALU.add))
    ve(lambda v: v.reduce_max(gm[:, :], bass.AP(gwork, 0, [[NTO * NE, P], [NE, NTO], [1, NE]]),
                              axis=AX.X))
    for t in range(NTO):
        ve(lambda v, t=t: v.tensor_scalar(out=gmask[:, t * NE:(t + 1) * NE],
                                          in0=gprob[:, t * NE:(t + 1) * NE],
                                          scalar1=gm[:, t:t + 1], scalar2=None, op0=ALU.is_ge))
    ve(lambda v: v.tensor_mul(gwork[:, :], gprob[:, :], gmask[:, :]))
    m_cf = ve(lambda v: v.tensor_copy(
        bass.AP(coefb, 0, [[NTO * 17, P], [17, NTO], [1, NE]]),
        bass.AP(gwork, 0, [[NTO * NE, P], [NE, NTO], [1, NE]])))
    m_sg = se(lambda sc: sc.activation(
        out=bass.AP(coefb, 16, [[NTO * 17, P], [17, NTO], [1, 1]]),
        in_=bass.AP(glog, 16, [[512, P], [64, NTO], [1, 1]]),
        func=AF.Sigmoid))
    if DEBUG:
        sywait(m_cf)
        sywait(m_sg)
        sy_dma(dbg["coef"][:, :], coefb[:, :])

    def bcast_col(col_idx, dst, rdy_marks):
        tname = "T1" if tpo_rot[0] % 2 == 0 else "T2"
        Tp = T1_ps if tname == "T1" else T2_ps
        tpo_rot[0] += 1
        slot_acquire(tname)
        for m in rdy_marks:
            pwait(m)
        m_f = None
        for tcc in range(NTO):
            fn = (lambda e, Tp=Tp, tcc=tcc, col_idx=col_idx:
                  e.transpose(Tp[0:1, tcc * P:(tcc + 1) * P],
                              bass.AP(coefb, tcc * 17 + col_idx, [[NTO * 17, P], [1, 1]]),
                              ident[:, :]))
            if tcc == NTO - 1:
                m_f = pe(fn)
                slot_fill(tname, m_f)
            else:
                pe_raw(fn)
        vwait(m_f)
        m_cr = ve(lambda v, Tp=Tp: v.tensor_copy(crow[0:1, :], Tp[0:1, 0:TO]))
        slot_evac(tname, m_cr)
        m_d = None
        for tb in range(2):
            sname = ["DN0", "DN1"][dn_rot[0] % 2]
            dn_rot[0] += 1
            ps = M_ps[:, 0:512] if sname == "DN0" else M_ps[:, 512:1024]
            slot_acquire(sname)
            if tb == 0:
                pwait(m_cr)
            fn = (lambda e, ps=ps, tb=tb:
                  e.matmul(ps, ones_rb[0:1, :],
                           crow[0:1, tb * 512:(tb + 1) * 512], start=True, stop=True))
            m_b = pe(fn)
            slot_fill(sname, m_b)
            vwait(m_b)
            m_d = ve(lambda v, ps=ps, dst=dst, tb=tb: v.tensor_copy(
                dst[:, tb * 512:(tb + 1) * 512], ps))
            slot_evac(sname, m_d)
        return m_d

    hb_war = [None, None]
    ffn_first = {}
    dn_marks = []

    def up_block(c0s_g, c0s_u, hdst, jj, m_scale_rdy, scale_buf):
        sname = ["SA", "SB"][up_rot[0] % 2]
        up_rot[0] += 1
        ps = S_ps[:, 0:1024] if sname == "SA" else S_ps[:, 1024:2048]
        slot_acquire(sname)
        m_f = None
        for di in range(NDC):
            w_ap = rv(c0s_g[di], P)
            for tb in range(2):
                fn = (lambda e, ps=ps, tb=tb, di=di, w_ap=w_ap:
                      e.matmul(ps[:, tb * 512:(tb + 1) * 512], w_ap,
                               ln2v(di, tb * 512, 512),
                               start=(di == 0), stop=(di == NDC - 1)))
                if di == NDC - 1 and tb == 1:
                    m_f = pe(fn)
                    slot_fill(sname, m_f)
                else:
                    pe_raw(fn)
            rdone(c0s_g[di] + P)
        swait(m_f)
        m_sgl = se(lambda sc, ps=ps: sc.activation(out=sgl[:, :], in_=ps[:, :], func=AF.Silu))
        slot_evac(sname, m_sgl)
        vwait(m_sgl)
        if m_scale_rdy is not None:
            vwait(m_scale_rdy)
        m_sgl2 = ve(lambda v, scale_buf=scale_buf: v.tensor_mul(sgl2[:, :], sgl[:, :], scale_buf[:, :]))
        sname2 = ["SA", "SB"][up_rot[0] % 2]
        up_rot[0] += 1
        ps2 = S_ps[:, 0:1024] if sname2 == "SA" else S_ps[:, 1024:2048]
        slot_acquire(sname2)
        m_f2 = None
        for di in range(NDC):
            w_ap = rv(c0s_u[di], P)
            for tb in range(2):
                fn = (lambda e, ps2=ps2, tb=tb, di=di, w_ap=w_ap:
                      e.matmul(ps2[:, tb * 512:(tb + 1) * 512], w_ap,
                               ln2v(di, tb * 512, 512),
                               start=(di == 0), stop=(di == NDC - 1)))
                if di == NDC - 1 and tb == 1:
                    m_f2 = pe(fn)
                    slot_fill(sname2, m_f2)
                else:
                    pe_raw(fn)
            rdone(c0s_u[di] + P)
        vwait(m_f2)
        m_h = ve(lambda v, ps2=ps2, hdst=hdst, jj=jj: v.scalar_tensor_tensor(
            out=hdst[:, jj * 1024:(jj + 1) * 1024], in0=ps2[:, :], scalar=1.0,
            in1=sgl2[:, :], op0=ALU.mult, op1=ALU.mult))
        slot_evac(sname2, m_h)
        return m_h

    def down_block(c0s_dn, hsrc, m_h_last, nic):
        m_fill_last = None
        for dc in range(NDC):
            for tb in range(2):
                sname = ["DN0", "DN1"][dn_rot[0] % 2]
                dn_rot[0] += 1
                ps = M_ps[:, 0:512] if sname == "DN0" else M_ps[:, 512:1024]
                slot_acquire(sname)
                if dc == 0 and tb == 0:
                    pwait(m_h_last)
                m_f = None
                for i in range(nic):
                    w_ap = rv(c0s_dn[(dc, i)], P)
                    fn = (lambda e, ps=ps, i=i, tb=tb, w_ap=w_ap, hsrc=hsrc:
                          e.matmul(ps, w_ap,
                                   hsrc[:, i * 1024 + tb * 512: i * 1024 + tb * 512 + 512],
                                   start=(i == 0), stop=(i == nic - 1)))
                    if i == nic - 1:
                        m_f = pe(fn)
                        slot_fill(sname, m_f)
                    else:
                        pe_raw(fn)
                    if tb == 1:
                        rdone(c0s_dn[(dc, i)] + P)
                vwait(m_f)
                key = (dc, tb)
                dst = ffn_a[:, dc * TO + tb * 512: dc * TO + tb * 512 + 512]
                if key not in ffn_first:
                    ffn_first[key] = True
                    m_a = ve(lambda v, dst=dst, ps=ps: v.tensor_copy(dst, ps))
                else:
                    m_a = ve(lambda v, dst=dst, ps=ps: v.scalar_tensor_tensor(
                        out=dst, in0=ps, scalar=1.0, in1=dst, op0=ALU.mult, op1=ALU.add))
                dn_marks.append(m_a)
                slot_evac(sname, m_a)
                m_fill_last = m_f
        return m_fill_last

    if DEBUG:
        vwait(m_dao)
        vwait(m_dof)
    m_sgb = bcast_col(16, sgb, [m_cf, m_sg])

    for blk in range(4):
        hcur = hb[blk % 2]
        c0s_g = {}
        c0s_u = {}
        m_h = None
        for j in range(NI):
            oc_g = blk * NI + j
            oc_u = 44 + blk * NI + j
            c0s_g = [t_.tile(sguT[di * P:(di + 1) * P, oc_g * P:(oc_g + 1) * P]) for di in range(NDC)]
            c0s_u = [t_.tile(sguT[di * P:(di + 1) * P, oc_u * P:(oc_u + 1) * P]) for di in range(NDC)]
            if hb_war[blk % 2] is not None and j == 0:
                vwait(hb_war[blk % 2])
            m_h = up_block(c0s_g, c0s_u, hcur, j, m_sgb if (blk == 0 and j == 0) else None, sgb)
        c0s_dn = {}
        for dc in range(NDC):
            for i in range(NI):
                icg = blk * NI + i
                c0s_dn[(dc, i)] = t_.tile(sdwT[icg * P:(icg + 1) * P, dc * P:(dc + 1) * P])
        m_pe_dn = down_block(c0s_dn, hcur, m_h, NI)
        hb_war[blk % 2] = m_pe_dn

    for e in range(NE):
        cb = cbb[e % 2]
        m_cb = bcast_col(e, cb, [m_cf])
        hcur = hb[e % 2]
        m_h = None
        for j in range(NI):
            c0s_g = [t_.tile(w13T[e][di * P:(di + 1) * P, j * P:(j + 1) * P]) for di in range(NDC)]
            c0s_u = [t_.tile(w13T[e][di * P:(di + 1) * P, (NI + j) * P:(NI + j + 1) * P]) for di in range(NDC)]
            if hb_war[e % 2] is not None and j == 0:
                vwait(hb_war[e % 2])
            m_h = up_block(c0s_g, c0s_u, hcur, j, m_cb if j == 0 else None, cb)
        c0s_dn = {}
        for dc in range(NDC):
            for i in range(NI):
                c0s_dn[(dc, i)] = t_.tile(w2T[e][i * P:(i + 1) * P, dc * P:(dc + 1) * P])
        m_pe_dn = down_block(c0s_dn, hcur, m_h, NI)
        hb_war[e % 2] = m_pe_dn

    slot_acquire("DN0")
    slot_acquire("DN1")
    swait(dn_marks[-1])
    pwait(dn_marks[-1])
    m_st = None
    for dc in range(NDC):
        m_st = sum_mms(ffn_a[:, dc * TO:(dc + 1) * TO], dc, NDC)
    rows_evac(rowA, m_st)
    slot_acquire("DN0")
    slot_acquire("DN1")
    sqm3 = []
    for dc in range(NDC):
        if dc >= 2:
            swait(sqm3[dc - 2][1])
        m_q = se(lambda sc, dc=dc: sc.activation(
            out=sqc[dc % 2][:, :], in_=ffn_a[:, dc * TO:(dc + 1) * TO], func=AF.Square))
        pwait(m_q)
        m_st = sum_mms(sqc[dc % 2][:, :], dc, NDC)
        sqm3.append((m_q, m_st))
    rows_evac(rowB, m_st)
    m_bc3 = moments_and_bcast()
    out_marks = []
    y2_marks = []
    sywait(dn_marks[-1])
    for dc in range(NDC):
        if dc >= 2:
            sywait(y2_marks[dc - 2])
        m_yr = sy_dma(yrb[dc % 2][:, :], y_spill[dc * P:(dc + 1) * P, :])
        if dc == 0:
            vwait(m_bc3)
        ve(lambda v, dc=dc: v.tensor_sub(tt2[:, :], ffn_a[:, dc * TO:(dc + 1) * TO], mbb[:, :]))
        ve(lambda v: v.tensor_mul(tt2[:, :], tt2[:, :], rbb[:, :]))
        vwait(m_yr)
        if dc >= 2:
            vwait((S["xdma"], out_marks[dc - 2][1]))
        m_y2 = ve(lambda v, dc=dc: v.tensor_add(y2b[dc % 2][:, :], tt2[:, :], yrb[dc % 2][:, :]))
        y2_marks.append(m_y2)
        sywait(m_y2)
        out_marks.append(sy_dma(y_out[dc * P:(dc + 1) * P, :], y2b[dc % 2][:, :]))
    sywait((S["xdma"], S["xdma"].n))

    tape_np = t_.finish()
    nch = tape_np.shape[1] // SLOT
    tape_d = nc.dram_tensor("wtape", [P, tape_np.shape[1]], F32, kind="ExternalInput").ap()
    for c in range(nch):
        if c >= RING:
            mk = chunk_rel.get(c - RING)
            assert mk is not None, ("unreleased chunk", c - RING, rstt)
            steps["gp"].append(lambda g, h=mk[0].h, v=mk[1]: g.wait_ge(h, v))
        S["ring_ld"].n += 16
        steps["gp"].append(lambda g, c=c: g.dma_start(
            out=ring[:, (c % RING) * SLOT:(c % RING + 1) * SLOT],
            in_=tape_d[:, c * SLOT:(c + 1) * SLOT]).then_inc(S["ring_ld"].h, 16))
    assert rstt["waited"] <= nch

    with nc.Block() as blk:
        @blk.tensor
        def _(t):
            for fn in steps["pe"]:
                fn(t)

        @blk.vector
        def _(v):
            for fn in steps["ve"]:
                fn(v)

        @blk.scalar
        def _(sc):
            for fn in steps["se"]:
                fn(sc)

        @blk.gpsimd
        def _(g):
            for fn in steps["gp"]:
                fn(g)

        @blk.sync
        def _(sy):
            for fn in steps["sy"]:
                fn(sy)

    ninstr = sum(len(v) for v in steps.values())
    print(f"build: {ninstr} scheduled steps; tape cols={tape_np.shape[1]} "
          f"({tape_np.nbytes / 1e6:.0f} MB); sems: "
          f"PE={S['PE'].n} VE={S['VE'].n} SE={S['SE'].n} xdma={S['xdma'].n // 16} ld={S['ring_ld'].n // 16}")
    return nc, tape_np


def kernel(L0_attn_x, L0_attn_w_qkv, L0_attn_q_norm_w, L0_attn_k_norm_w,
           L0_attn_w_proj, L0_attn_b_proj, L0_ffn_gate_w, L0_ffn_sgate_w,
           L0_ffn_w13, L0_ffn_w2, L0_ffn_shared_gate_up_w, L0_ffn_shared_down_w):
    X = np.asarray(L0_attn_x, dtype=np.float32)
    W = {
        "w_qkv": np.asarray(L0_attn_w_qkv, np.float32),
        "w_proj": np.asarray(L0_attn_w_proj, np.float32),
        "gate_w": np.asarray(L0_ffn_gate_w, np.float32),
        "sgate_w": np.asarray(L0_ffn_sgate_w, np.float32),
        "shared_gate_up_w": np.asarray(L0_ffn_shared_gate_up_w, np.float32),
        "shared_down_w": np.asarray(L0_ffn_shared_down_w, np.float32),
        "w13": np.asarray(L0_ffn_w13, np.float32),
        "w2": np.asarray(L0_ffn_w2, np.float32),
    }
    bias_t = np.ascontiguousarray(
        np.asarray(L0_attn_b_proj, np.float32).reshape(NDC, P).T)
    qkw = np.ascontiguousarray(np.stack(
        [np.asarray(L0_attn_q_norm_w, np.float32),
         np.asarray(L0_attn_k_norm_w, np.float32)], axis=1))
    nc, tape_np = build(W)
    in_maps = []
    for c in range(8):
        b, half = c // 2, c % 2
        t0 = half * TO
        xb = X[b]
        x_rot = np.concatenate([xb[t0:t0 + TO], xb[TO - t0: 2 * TO - t0]], axis=0)
        in_maps.append({
            "x_tm": np.ascontiguousarray(x_rot),
            "x_fm": np.ascontiguousarray(xb[t0:t0 + TO].T),
            "bias_in": bias_t,
            "qkw_in": qkw,
            "wtape": tape_np,
        })
    res = run_bass_kernel_spmd(nc, in_maps, core_ids=list(range(8)))
    out = np.empty((4, L, DM), dtype=np.float32)
    for c in range(8):
        b, half = c // 2, c % 2
        t0 = half * TO
        out[b, t0:t0 + TO, :] = res.results[c]["y_out"].T
    return out
